# version 14
# speedup vs baseline: 1.0532x; 1.0002x over previous
import numpy as np
import ml_dtypes
from contextlib import ExitStack

import concourse.bass as bass
import concourse.mybir as mybir
from concourse.bass_utils import run_bass_kernel_spmd
from concourse.bass_core_rust import IndirectOffsetOnAxis

F32 = mybir.dt.float32
BF16 = mybir.dt.bfloat16
I32 = mybir.dt.int32
AF = mybir.ActivationFunctionType
ALU = mybir.AluOpType
AXX = mybir.AxisListType.X

NCORES = 8
B, T, D, NH, DH, II = 4, 4096, 1024, 16, 64, 2816
TOK = 2048
FC = D // 128
ICH = II // 128
TC = TOK // 512
NTT = TOK // 128
SEQC = T // 128
QMAX = 127.0
STAGE = 3

IX_QKW = 0
IX_JRK = 8
IX_VR = 16
NIDX = 16 + 128


def build(stage=STAGE):
    nc = bass.Bass("TRN2", target_bir_lowering=False, debug=False,
                   detect_race_conditions=False)

    def din(name, shape, dt=F32):
        return nc.dram_tensor(name, shape, dt, kind="ExternalInput").ap()

    xT = din("xT", [D, TOK])
    wgT = din("wgT", [D, II]); wuT = din("wuT", [D, II]); wdT = din("wdT", [II, D])
    wg_nat = din("wg_nat", [II, D]); wu_nat = din("wu_nat", [II, D]); wd_nat = din("wd_nat", [D, II])
    wqT = din("wqT", [D, D]); wkT = din("wkT", [D, D]); wvT = din("wvT", [D, D]); woT = din("woT", [D, D])
    bqc = din("bqc", [128, FC]); bkc = din("bkc", [128, FC]); boc = din("boc", [128, FC])
    bvr = din("bvr", [1, D])
    clip2 = din("clip2", [1, 2])
    cctab = din("cctab", [128, TOK], BF16); sstab = din("sstab", [128, TOK], BF16)
    masks = din("masks", [128, 4 * 512], BF16)
    identf = din("identf", [128, 128])
    ones_b = din("ones_b", [128, 128], BF16)
    ones_f = din("ones_f", [128, 128])
    permb = din("permb", [128, 128], BF16)
    itbl = din("itbl", [128, 4], I32)

    out = nc.dram_tensor("out", [D, TOK], F32, kind="ExternalOutput").ap()
    dbg = nc.dram_tensor("dbg", [128, 32], F32, kind="ExternalOutput").ap()
    dbg2 = nc.dram_tensor("dbg2", [128, 512], BF16, kind="ExternalOutput").ap()
    dbg3 = nc.dram_tensor("dbg3", [128, 128], BF16, kind="ExternalOutput").ap()
    dbg4 = nc.dram_tensor("dbg4", [128, 512], BF16, kind="ExternalOutput").ap()
    dbg5 = nc.dram_tensor("dbg5", [128, 256], BF16, kind="ExternalOutput").ap()
    dbg6 = nc.dram_tensor("dbg6", [128, 512], F32, kind="ExternalOutput").ap()
    dbg7 = nc.dram_tensor("dbg7", [128, 512], BF16, kind="ExternalOutput").ap()
    dbg8 = nc.dram_tensor("dbg8", [128, 8 * 128], BF16, kind="ExternalOutput").ap()
    dbg9 = nc.dram_tensor("dbg9", [128, 1024], BF16, kind="ExternalOutput").ap()
    dbg10 = nc.dram_tensor("dbg10", [128, 512], BF16, kind="ExternalOutput").ap()

    hspill = nc.dram_tensor("hspill", [II, TOK], BF16).ap()
    wgq_d = nc.dram_tensor("wgq_d", [ICH, FC, 128, 128], BF16).ap()
    wuq_d = nc.dram_tensor("wuq_d", [ICH, FC, 128, 128], BF16).ap()
    wdq_d = nc.dram_tensor("wdq_d", [FC, ICH, 128, 128], BF16).ap()
    agq_in_q = nc.dram_tensor("agq_in_q", [FC * 128, TOK], BF16).ap()
    agq_in_k = nc.dram_tensor("agq_in_k", [FC * 128, TOK], BF16).ap()
    agq_in_v = nc.dram_tensor("agq_in_v", [FC * TOK, 128], BF16).ap()
    agq_out_q = nc.dram_tensor("agq_out_q", [2 * FC * 128, TOK], BF16).ap()
    agq_out_k = nc.dram_tensor("agq_out_k", [2 * FC * 128, TOK], BF16).ap()
    agq_out_v = nc.dram_tensor("agq_out_v", [2 * FC * TOK, 128], BF16).ap()
    ago_in = nc.dram_tensor("ago_in", [4 * 128, T], BF16).ap()
    ago_outs = [nc.dram_tensor(f"ago_out{j}", [2 * 128, T], BF16).ap()
                for j in range(4)]
    ncc = 6
    ccio = [(nc.dram_tensor(f"cc{i}i", [1, 8], F32).ap(),
             nc.dram_tensor(f"cc{i}o", [1, 8], F32).ap()) for i in range(ncc)]

    ar_views = [a.rearrange("r (s u) -> (r s) u", u=TOK) for a in ago_outs]

    uid = [0]

    def nm(p):
        uid[0] += 1
        return f"{p}{uid[0]}"

    with ExitStack() as prog:
        E = prog.enter_context
        r = E(nc.sbuf_tensor("r", [128, FC * TOK], F32))
        B1 = E(nc.sbuf_tensor("B1", [128, FC * TOK], BF16))
        msk = E(nc.sbuf_tensor("msk", [128, 4 * 512], BF16))
        idf = E(nc.sbuf_tensor("idf", [128, 128], F32))
        onb = E(nc.sbuf_tensor("onb", [128, 128], BF16))
        onf = E(nc.sbuf_tensor("onf", [128, 128], F32))
        psw = E(nc.sbuf_tensor("psw", [128, 128], BF16))
        bq_sb = E(nc.sbuf_tensor("bq_sb", [128, FC], F32))
        bk_sb = E(nc.sbuf_tensor("bk_sb", [128, FC], F32))
        bo_sb = E(nc.sbuf_tensor("bo_sb", [128, FC], F32))
        bv_bc = E(nc.sbuf_tensor("bv_bc", [128, D], BF16))
        sigx = E(nc.sbuf_tensor("sigx", [1, 2], F32))
        swg = E(nc.sbuf_tensor("swg", [128, ICH], F32))
        swu = E(nc.sbuf_tensor("swu", [128, ICH], F32))
        swd = E(nc.sbuf_tensor("swd", [128, FC], F32))
        sgc = E(nc.sbuf_tensor("sgc", [128, ICH], F32))
        suc = E(nc.sbuf_tensor("suc", [128, ICH], F32))
        sdc = E(nc.sbuf_tensor("sdc", [128, FC], F32))
        invcol = E(nc.sbuf_tensor("invcol", [128, 2], F32))
        srow = E(nc.sbuf_tensor("srow", [128, 8], F32))
        umax = E(nc.sbuf_tensor("umax", [128, 1], F32))
        lmax = E(nc.sbuf_tensor("lmax", [1, 4], F32))
        gmv = E(nc.sbuf_tensor("gmv", [1, 1], F32))
        zed = E(nc.sbuf_tensor("zed", [1, 8], F32))
        tb = E(nc.sbuf_tensor("tb", [128, 4], I32))
        idxm = E(nc.sbuf_tensor("idxm", [128, NIDX], I32))
        abc = E(nc.sbuf_tensor("abc", [128, 2 * 512], F32))

        DHs = E(nc.semaphore("DHs"))
        DGs = E(nc.semaphore("DGs"))
        SVs = E(nc.semaphore("SVs"))
        SSs = E(nc.semaphore("SSs"))
        STs = E(nc.semaphore("STs"))
        CCs = E(nc.semaphore("CCs"))
        block = E(nc.Block())

        C = {DHs: 0, DGs: 0, SVs: 0, SSs: 0, STs: 0, CCs: 0}
        ALLS = [DHs, DGs, SVs, SSs, STs, CCs]

        def inc(ins, sem, n=None):
            n = n if n is not None else (16 if sem in (DHs, DGs) else 1)
            ins.then_inc(sem, n)
            C[sem] += n
            return C[sem]

        def wt(eng, *sems, at=None):
            for s in sems:
                eng.wait_ge(s, C[s] if at is None else at[s])

        def wta(eng, bar):
            for s in ALLS:
                if bar[s] > 0:
                    eng.wait_ge(s, bar[s])

        snap = lambda: dict(C)

        def ckpt(col, src_ap):
            b = snap()

            @block.vector
            def _(v, col=col, src_ap=src_ap, b=b):
                wta(v, b)
                inc(v.tensor_copy(abc[:, col:col + 1], src_ap), SVs)

        @block.sync
        def _(sy):
            inc(sy.dma_start(out=msk[:], in_=masks), DHs)
            inc(sy.dma_start(out=idf[:], in_=identf), DHs)
            inc(sy.dma_start(out=onb[:], in_=ones_b), DHs)
            inc(sy.dma_start(out=onf[:], in_=ones_f), DHs)
            inc(sy.dma_start(out=psw[:], in_=permb), DHs)
            inc(sy.dma_start(out=bq_sb[:], in_=bqc), DHs)
            inc(sy.dma_start(out=bk_sb[:], in_=bkc), DHs)
            inc(sy.dma_start(out=bo_sb[:], in_=boc), DHs)
            inc(sy.dma_start(out=srow[0:1, 0:2], in_=clip2), DHs)
            inc(sy.dma_start(out=tb[:], in_=itbl), DHs)
            for fc in range(FC):
                inc(sy.dma_start(out=r[:, fc * TOK:(fc + 1) * TOK],
                                 in_=xT[fc * 128:(fc + 1) * 128, :]), DHs)
            sy.wait_ge(DHs, C[DHs])
        p0 = snap()

        @block.scalar
        def _(s):
            wt(s, DHs, at=p0)
            inc(s.activation(sigx[0:1, 0:2], srow[0:1, 0:2], AF.Sigmoid), SSs)

        @block.vector
        def _(v):
            wt(v, DHs, at=p0)
            v.memset(zed[:], 0.0)
            v.memset(srow[:, 2:8], 0.0)
            v.memset(abc[:], 0.0)
            v.memset(lmax[:], 0.0)
            v.memset(srow[:, 6:7], 1e-5)
            for g in range(FC):
                v.tensor_scalar(idxm[:, IX_QKW + g:IX_QKW + g + 1], tb[:, 0:1],
                                256 * (g // 4), None, ALU.add)
            for j in range(4):
                for rr in range(2):
                    k = IX_JRK + 2 * j + rr
                    v.tensor_scalar(idxm[:, k:k + 1], tb[:, 2:3],
                                    128 * j + 1024 * rr, None, ALU.add)
            for cch in range(32):
                for j in range(4):
                    rr = cch // 16
                    k = IX_VR + 4 * cch + j
                    v.tensor_scalar(idxm[:, k:k + 1], tb[:, 3:4],
                                    16384 * rr + 2048 * j + 128 * (cch % 16),
                                    None, ALU.add)
            inc(v.engine_nop(), SVs)

        @block.gpsimd
        def _(g):
            wt(g, SVs)
            for i in range(ncc):
                inc(g.dma_start(out=ccio[i][0][0:1, :], in_=zed[:]), DGs)
            g.wait_ge(DGs, C[DGs])
        p0b = snap()

        def emit_ln():
            with nc.sbuf_tensor(nm("xb"), [128, 512], BF16) as xb, \
                 nc.sbuf_tensor(nm("sq"), [128, 512], BF16) as sq, \
                 nc.sbuf_tensor(nm("row"), [1, 6 * 512], F32) as row, \
                 nc.sbuf_tensor(nm("tmpf"), [128, 512], F32) as tmpf, \
                 nc.psum_tensor(nm("ps_sx"), [1, 512], F32) as ps_sx, \
                 nc.psum_tensor(nm("ps_sq"), [1, 512], F32) as ps_sq, \
                 nc.psum_tensor(nm("ps_ab"), [128, 512], F32) as ps_ab:
                bar = snap()

                @block.vector
                def _(v):
                    wta(v, bar)
                    inc(v.memset(umax[:], 0.0), SVs)
                svm = {}
                for tc in range(TC):
                    for fc in range(FC):
                        sl = slice(fc * TOK + tc * 512, fc * TOK + tc * 512 + 512)
                        mprev = snap()

                        @block.scalar
                        def _(s, sl=sl, mprev=mprev):
                            wt(s, STs, at=mprev)
                            wta(s, bar)
                            s.copy(xb[:, :], r[:, sl])
                            inc(s.activation(sq[:, :], r[:, sl], AF.Square), SSs)
                        m1 = snap()

                        @block.tensor
                        def _(t, fc=fc, tc=tc, m1=m1,
                              svp=svm.get(tc - 1, 0)):
                            wt(t, SSs, at=m1)
                            if fc == 0 and tc >= 1:
                                t.wait_ge(SVs, svp)
                            t.matmul(ps_sx[:, :], onb[:, 0:1], xb[:, :],
                                     start=(fc == 0), stop=(fc == FC - 1))
                            inc(t.matmul(ps_sq[:, :], onb[:, 0:1], sq[:, :],
                                         start=(fc == 0), stop=(fc == FC - 1)), STs)
                    ms = snap()

                    @block.vector
                    def _(v, ms=ms):
                        wt(v, STs, at=ms)
                        mean = row[0:1, 0:512]; ex2 = row[0:1, 512:1024]
                        var = row[0:1, 1024:1536]; m2r = row[0:1, 2560:3072]
                        inc(v.tensor_scalar(mean, ps_sx[0:1, :], 1.0 / D, None, ALU.mult), SVs)
                        inc(v.tensor_scalar(ex2, ps_sq[0:1, :], 1.0 / D, None, ALU.mult), SVs)
                        v.wait_ge(SVs, C[SVs])
                        v.tensor_mul(m2r, mean, mean)
                        inc(v.tensor_sub(var, ex2, m2r), SVs)
                    svm[tc] = C[SVs]
                    mv = snap()

                    @block.scalar
                    def _(s, mv=mv):
                        wt(s, SVs, at=mv)
                        inc(s.activation(row[0:1, 1536:2048], row[0:1, 1024:1536],
                                         AF.Sqrt, bias=srow[0:1, 6:7], scale=1.0), SSs)
                    msq = snap()

                    @block.vector
                    def _(v, msq=msq):
                        wt(v, SSs, at=msq)
                        rstd = row[0:1, 1024:1536]
                        inc(v.reciprocal(rstd, row[0:1, 1536:2048]), SVs)
                        v.wait_ge(SVs, C[SVs])
                        inc(v.scalar_tensor_tensor(row[0:1, 2048:2560], row[0:1, 0:512],
                                                   -1.0, rstd, ALU.mult, ALU.mult), SVs)
                    mr = snap()

                    @block.tensor
                    def _(t, mr=mr):
                        wt(t, SVs, at=mr)
                        inc(t.matmul(ps_ab[:, :], onf[0:1, 0:128], row[0:1, 1024:1536],
                                     start=True, stop=True), STs)
                    mt1 = snap()

                    @block.vector
                    def _(v, mt1=mt1):
                        wt(v, STs, at=mt1)
                        inc(v.tensor_copy(abc[:, 0:512], ps_ab[:, :]), SVs)
                        v.wait_ge(SVs, C[SVs])
                    mt1v = snap()

                    @block.tensor
                    def _(t, mt1v=mt1v):
                        wt(t, SVs, at=mt1v)
                        inc(t.matmul(ps_ab[:, :], onf[0:1, 0:128], row[0:1, 2048:2560],
                                     start=True, stop=True), STs)
                    mt2 = snap()

                    @block.vector
                    def _(v, tc=tc, mt2=mt2):
                        wt(v, STs, at=mt2)
                        inc(v.tensor_copy(abc[:, 512:1024], ps_ab[:, :]), SVs)
                        v.wait_ge(SVs, C[SVs])
                        for fc in range(FC):
                            sl = slice(fc * TOK + tc * 512, fc * TOK + tc * 512 + 512)
                            v.tensor_mul(tmpf[:, :], r[:, sl], abc[:, 0:512])
                            v.tensor_add(B1[:, sl], tmpf[:, :], abc[:, 512:1024])
                            inc(v.reduce_max(srow[:, 2:3], B1[:, sl], axis=AXX,
                                             apply_absolute_value=True), SVs)
                            v.wait_ge(SVs, C[SVs])
                            v.tensor_max(umax[:], umax[:], srow[:, 2:3])
                        inc(v.engine_nop(), SVs)

        def emit_gmax(cc_idx, sig_col):
            cin, cout = ccio[cc_idx]
            with nc.psum_tensor(nm("ps_gm"), [1, 128], F32) as ps_gm, \
                 nc.psum_tensor(nm("ps_gb"), [128, 1], F32) as ps_gb:
                m0 = snap()

                @block.tensor
                def _(t, m0=m0):
                    wt(t, SVs, at=m0)
                    inc(t.transpose(ps_gm[:, :], umax[:, 0:1], idf[:]), STs)
                m1 = snap()

                @block.vector
                def _(v, m1=m1):
                    wt(v, STs, at=m1)
                    inc(v.reduce_max(lmax[0:1, 0:1], ps_gm[0:1, :], axis=AXX), SVs)
                    v.wait_ge(SVs, C[SVs])
                m2 = snap()

                @block.gpsimd
                def _(g, m2=m2):
                    wt(g, SVs, at=m2)
                    inc(g.dma_start(out=cin[0:1, 0:1], in_=lmax[0:1, 0:1]), DGs)
                    wt(g, DGs)
                    inc(g.collective_compute(
                        "AllReduce", ALU.max, replica_groups=[list(range(NCORES))],
                        ins=[cin], outs=[cout]), CCs)
                    wt(g, CCs)
                    inc(g.dma_start(out=gmv[0:1, 0:1], in_=cout[0:1, 0:1]), DGs)
                m3 = snap()

                @block.vector
                def _(v, sig_col=sig_col, m3=m3):
                    wt(v, DGs, at=m3)
                    wt(v, SSs)
                    inc(v.tensor_mul(lmax[0:1, 1:2], gmv[0:1, 0:1],
                                     sigx[0:1, sig_col:sig_col + 1]), SVs)
                    v.wait_ge(SVs, C[SVs])
                    inc(v.reciprocal(srow[0:1, 4:5], lmax[0:1, 1:2]), SVs)
                    v.wait_ge(SVs, C[SVs])
                    inc(v.tensor_scalar(srow[0:1, 4:5], srow[0:1, 4:5], QMAX, None,
                                        ALU.mult), SVs)
                m4 = snap()

                @block.tensor
                def _(t, m4=m4):
                    wt(t, SVs, at=m4)
                    inc(t.matmul(ps_gb[:, :], onf[0:1, 0:128], srow[0:1, 4:5],
                                 start=True, stop=True), STs)
                m5 = snap()

                @block.vector
                def _(v, sig_col=sig_col, m5=m5):
                    wt(v, STs, at=m5)
                    inc(v.tensor_copy(invcol[:, sig_col:sig_col + 1], ps_gb[:, :]), SVs)
                    v.wait_ge(SVs, C[SVs])

        def emit_scale_cols(scol, n, dst, extra):
            with nc.psum_tensor(nm("ps_sc"), [128, 1], F32) as ps_sc:
                m1 = snap()

                @block.tensor
                def _(t, m1=m1):
                    wt(t, SVs, at=m1)
                    inc(t.matmul(ps_sc[:, :], onf[0:1, 0:128], lmax[0:1, 1:2],
                                 start=True, stop=True), STs)
                m2 = snap()

                @block.vector
                def _(v, scol=scol, n=n, dst=dst, extra=extra, m2=m2):
                    wt(v, STs, at=m2)
                    inc(v.tensor_copy(srow[:, 5:6], ps_sc[:, :]), SVs)
                    v.wait_ge(SVs, C[SVs])
                    inc(v.tensor_scalar(dst[:, 0:n], scol[:, 0:n], srow[:, 5:6],
                                        extra / (QMAX * QMAX), ALU.mult, ALU.mult), SVs)

        def emit_quant_B1(inv_idx):
            with nc.sbuf_tensor(nm("q32"), [128, 512], F32) as q32, \
                 nc.sbuf_tensor(nm("qi"), [128, 512], I32) as qi:
                bar = snap()

                @block.vector
                def _(v):
                    wta(v, bar)
                    for fc in range(FC):
                        for tc in range(TC):
                            sl = slice(fc * TOK + tc * 512, fc * TOK + tc * 512 + 512)
                            v.tensor_scalar(qi[:, :], B1[:, sl],
                                            invcol[:, inv_idx:inv_idx + 1],
                                            None, ALU.mult)
                            v.tensor_scalar(B1[:, sl], qi[:, :], -128.0, 127.0,
                                            ALU.max, ALU.min)
                    inc(v.engine_nop(), SVs)

        def emit_weight_prep():
            for wT_ap, nat_ap, nn, nw, scol, sn, tch, tw, qd, kind in (
                    (wgT, wg_nat, ICH, D, swg, ICH, FC, II, wgq_d, "gu"),
                    (wuT, wu_nat, ICH, D, swu, ICH, FC, II, wuq_d, "gu"),
                    (wdT, wd_nat, FC, II, swd, FC, ICH, D, wdq_d, "dn")):
                with nc.sbuf_tensor(nm("natb"), [128, nw], F32) as natb, \
                     nc.sbuf_tensor(nm("qbuf"), [128, 2 * tw], F32) as qbuf, \
                     nc.sbuf_tensor(nm("invb"), [128, tw], F32) as invb, \
                     nc.sbuf_tensor(nm("invr2"), [1, 256], F32) as invr2, \
                     nc.sbuf_tensor(nm("qsi"), [128, 512], I32) as qsi, \
                     nc.sbuf_tensor(nm("cbuf"), [128, 2 * tw], BF16) as cbuf, \
                     nc.psum_tensor(nm("ps_t0"), [1, 128], F32) as ps_t0, \
                     nc.psum_tensor(nm("ps_t1"), [1, 128], F32) as ps_t1, \
                     nc.psum_tensor(nm("ps_b0"), [128, 128], F32) as ps_b0, \
                     nc.psum_tensor(nm("ps_b1"), [128, 128], F32) as ps_b1:
                    ps_b = [ps_b0, ps_b1]; ps_t = [ps_t0, ps_t1]
                    pbar = snap()
                    vmarks = []
                    dbuf = nw <= tw
                    for j in range(nn):
                        m0 = snap()

                        @block.sync
                        def _(sy, j=j, m0=m0, dbuf=dbuf):
                            if j >= (2 if dbuf else 1):
                                sy.wait_ge(SVs, vmarks[j - 2 if dbuf else j - 1])
                            else:
                                wta(sy, pbar)
                            dst = (qbuf[:, (j % 2) * tw:(j % 2) * tw + nw]
                                   if dbuf else natb[:, 0:nw])
                            inc(sy.dma_start(out=dst,
                                             in_=nat_ap[j * 128:(j + 1) * 128, :]), DHs)
                            sy.wait_ge(DHs, C[DHs])
                        ml = snap()

                        @block.vector
                        def _(v, j=j, ml=ml, dbuf=dbuf):
                            wt(v, DHs, at=ml)
                            src = (qbuf[:, (j % 2) * tw:(j % 2) * tw + nw]
                                   if dbuf else natb[:, 0:nw])
                            inc(v.reduce_max(scol[:, j:j + 1], src,
                                             axis=AXX, apply_absolute_value=True), SVs)
                            v.wait_ge(SVs, C[SVs])
                            vmarks.append(C[SVs])
                    minv0 = snap()

                    @block.vector
                    def _(v, sn=sn, minv0=minv0):
                        wt(v, SVs, at=minv0)
                        inc(v.reciprocal(natb[:, 0:sn], scol[:, 0:sn]), SVs)
                        v.wait_ge(SVs, C[SVs])
                        inc(v.tensor_scalar(natb[:, 0:sn], natb[:, 0:sn], QMAX, None,
                                            ALU.mult), SVs)
                        v.wait_ge(SVs, C[SVs])
                    mtc = snap()
                    bmarks, v1m, t2m = {}, {}, {}
                    for j in range(sn):
                        pb = j % 2

                        @block.tensor
                        def _(t, j=j, pb=pb, mtc=mtc):
                            wt(t, SVs, at=mtc)
                            if j >= 2:
                                t.wait_ge(SVs, v1m[j - 2])
                            inc(t.transpose(ps_t[pb][:, :], natb[:, j:j + 1],
                                            idf[:]), STs)
                        mt1 = C[STs]

                        @block.vector
                        def _(v, j=j, pb=pb, mt1=mt1):
                            v.wait_ge(STs, mt1)
                            if j >= 2:
                                v.wait_ge(STs, t2m[j - 2])
                            inc(v.tensor_copy(invr2[0:1, pb * 128:(pb + 1) * 128],
                                              ps_t[pb][:, :]), SVs)
                            v.wait_ge(SVs, C[SVs])
                            v1m[j] = C[SVs]

                        @block.tensor
                        def _(t, j=j, pb=pb):
                            t.wait_ge(SVs, v1m[j])
                            if j >= 2:
                                t.wait_ge(SVs, bmarks[j - 2])
                            inc(t.matmul(ps_b[pb][:, :], onf[0:1, 0:128],
                                         invr2[0:1, pb * 128:(pb + 1) * 128],
                                         start=True, stop=True), STs)
                            t2m[j] = C[STs]

                        @block.vector
                        def _(v, j=j, pb=pb):
                            v.wait_ge(STs, t2m[j])
                            inc(v.tensor_copy(invb[:, j * 128:(j + 1) * 128],
                                              ps_b[pb][:, :]), SVs)
                            v.wait_ge(SVs, C[SVs])
                            bmarks[j] = C[SVs]
                    qmarks, smarks = {}, {}
                    for jc in range(tch):
                        pq = jc % 2
                        mq0 = snap()

                        @block.sync
                        def _(sy, jc=jc, pq=pq, mq0=mq0):
                            if jc >= 2:
                                sy.wait_ge(SVs, qmarks[jc - 2])
                            else:
                                wt(sy, SVs, at=mq0)
                            inc(sy.dma_start(out=qbuf[:, pq * tw:(pq + 1) * tw],
                                             in_=wT_ap[jc * 128:(jc + 1) * 128, :]), DHs)
                            sy.wait_ge(DHs, C[DHs])
                        mq1 = snap()

                        @block.vector
                        def _(v, jc=jc, pq=pq, tw=tw, mq1=mq1):
                            wt(v, DHs, at=mq1)
                            wt(v, SVs, at=mq1)
                            if jc >= 2:
                                v.wait_ge(DGs, smarks[jc - 2])
                            for sb in range((tw + 511) // 512):
                                w = min(512, tw - sb * 512)
                                sl = slice(pq * tw + sb * 512, pq * tw + sb * 512 + w)
                                isl = slice(sb * 512, sb * 512 + w)
                                v.tensor_mul(qsi[:, 0:w], qbuf[:, sl], invb[:, isl])
                                v.tensor_scalar(cbuf[:, pq * tw + sb * 512:
                                                pq * tw + sb * 512 + w], qsi[:, 0:w],
                                                -128.0, 127.0, ALU.max, ALU.min)
                            inc(v.engine_nop(), SVs)
                            qmarks[jc] = C[SVs]

                        @block.gpsimd
                        def _(g, jc=jc, pq=pq, kind=kind, tw=tw, qd=qd):
                            g.wait_ge(SVs, qmarks[jc])
                            cv = cbuf[:, pq * tw:(pq + 1) * tw].rearrange(
                                "p (i c) -> p i c", c=128)
                            if kind == "gu":
                                inc(g.dma_start(
                                    out=qd[:, jc, :, :].rearrange("i p c -> p i c"),
                                    in_=cv), DGs)
                            else:
                                inc(g.dma_start(
                                    out=qd[:, jc, :, :].rearrange("o p c -> p o c"),
                                    in_=cv), DGs)
                            g.wait_ge(DGs, C[DGs])
                            smarks[jc] = C[DGs]

        def emit_ffn_a():
            with nc.sbuf_tensor(nm("wgs"), [128, 2 * FC * 128], BF16) as wgs, \
                 nc.sbuf_tensor(nm("wus"), [128, 2 * FC * 128], BF16) as wus, \
                 nc.sbuf_tensor(nm("silu"), [128, 2 * 512], BF16) as silu, \
                 nc.sbuf_tensor(nm("zbuf"), [128, 2 * 512], BF16) as zbuf, \
                 nc.sbuf_tensor(nm("tpre"), [128, 2 * 512], BF16) as tpre, \
                 nc.sbuf_tensor(nm("dbgstage"), [128, 512], F32) as dbgstage, \
                 nc.psum_tensor(nm("psg0"), [128, 512], F32) as psg0, \
                 nc.psum_tensor(nm("psg1"), [128, 512], F32) as psg1, \
                 nc.psum_tensor(nm("psu0"), [128, 512], F32) as psu0, \
                 nc.psum_tensor(nm("psu1"), [128, 512], F32) as psu1:
                psg = [psg0, psg1]; psu = [psu0, psu1]
                bar = snap()

                @block.vector
                def _(v):
                    wta(v, bar)
                    inc(v.memset(umax[:], 0.0), SVs)
                steps = [(it, tc) for it in range(ICH) for tc in range(TC)]
                tmark, smark, vmark, spmark, zmark = {}, {}, {}, {}, {}
                mlds = {}
                for n, (it, tc) in enumerate(steps):
                    pp = n % 2
                    ppw = it % 2

                    if tc == 0:
                        @block.sync
                        def _(sy, n=n, it=it, ppw=ppw):
                            if it >= 2:
                                sy.wait_ge(STs, tmark[n - TC - 1])
                            else:
                                wta(sy, bar)
                            inc(sy.dma_start(
                                out=wgs[:, ppw * FC * 128:(ppw + 1) * FC * 128],
                                in_=wgq_d[it, :, :, :].rearrange("f p c -> p f c")), DHs)
                            inc(sy.dma_start(
                                out=wus[:, ppw * FC * 128:(ppw + 1) * FC * 128],
                                in_=wuq_d[it, :, :, :].rearrange("f p c -> p f c")), DHs)
                            sy.wait_ge(DHs, C[DHs])
                        mlds[it] = snap()

                    @block.tensor
                    def _(t, n=n, tc=tc, pp=pp, ppw=ppw, mld=mlds[it]):
                        wta(t, bar)
                        wt(t, DHs, at=mld)
                        if n >= 2:
                            t.wait_ge(SVs, zmark[n - 2])
                            t.wait_ge(SVs, vmark[n - 2])
                        for fc in range(FC):
                            ins = t.matmul(psg[pp][:, :],
                                     wgs[:, ppw * FC * 128 + fc * 128:
                                         ppw * FC * 128 + fc * 128 + 128],
                                     B1[:, fc * TOK + tc * 512:fc * TOK + tc * 512 + 512],
                                     start=(fc == 0), stop=(fc == FC - 1))
                        inc(ins, STs)
                        for fc in range(FC):
                            ins = t.matmul(psu[pp][:, :],
                                     wus[:, ppw * FC * 128 + fc * 128:
                                         ppw * FC * 128 + fc * 128 + 128],
                                     B1[:, fc * TOK + tc * 512:fc * TOK + tc * 512 + 512],
                                     start=(fc == 0), stop=(fc == FC - 1))
                        inc(ins, STs)
                        tmark[n] = C[STs]

                    @block.vector
                    def _(v, n=n, it=it, pp=pp):
                        v.wait_ge(STs, tmark[n] - 1)
                        if n >= 2:
                            v.wait_ge(SSs, smark[n - 2])
                        inc(v.tensor_scalar(zbuf[:, pp * 512:(pp + 1) * 512],
                                            psg[pp][:, :], sgc[:, it:it + 1], None,
                                            ALU.mult), SVs)
                        v.wait_ge(SVs, C[SVs])
                        zmark[n] = C[SVs]

                    @block.scalar
                    def _(s, n=n, pp=pp):
                        s.wait_ge(SVs, zmark[n])
                        if n >= 2:
                            s.wait_ge(SVs, vmark[n - 2])
                        inc(s.activation(silu[:, pp * 512:(pp + 1) * 512],
                                         zbuf[:, pp * 512:(pp + 1) * 512],
                                         AF.Silu, bias=0.0, scale=1.0), SSs)
                        smark[n] = C[SSs]

                    @block.vector
                    def _(v, n=n, it=it, pp=pp):
                        v.wait_ge(STs, tmark[n])
                        v.wait_ge(SSs, smark[n])
                        if n >= 2:
                            v.wait_ge(DHs, spmark[n - 2])
                        inc(v.scalar_tensor_tensor(tpre[:, pp * 512:(pp + 1) * 512],
                                                   psu[pp][:, :], suc[:, it:it + 1],
                                                   silu[:, pp * 512:(pp + 1) * 512],
                                                   ALU.mult, ALU.mult), SVs)
                        v.wait_ge(SVs, C[SVs])
                        inc(v.reduce_max(srow[:, 3:4], tpre[:, pp * 512:(pp + 1) * 512],
                                         axis=AXX, apply_absolute_value=True), SVs)
                        v.wait_ge(SVs, C[SVs])
                        inc(v.tensor_max(umax[:], umax[:], srow[:, 3:4]), SVs)
                        vmark[n] = C[SVs]

                    @block.sync
                    def _(sy, n=n, tc=tc, it=it, pp=pp):
                        sy.wait_ge(SVs, vmark[n])
                        inc(sy.dma_start(out=hspill[it * 128:(it + 1) * 128,
                                                    tc * 512:(tc + 1) * 512],
                                         in_=tpre[:, pp * 512:(pp + 1) * 512]), DHs)
                        sy.wait_ge(DHs, C[DHs])
                        spmark[n] = C[DHs]

        def emit_ffn_b():
            with nc.sbuf_tensor(nm("hq"), [128, ICH * 512], BF16) as hq, \
                 nc.sbuf_tensor(nm("wdsF"), [128, FC * ICH * 128], BF16) as wdsF, \
                 nc.sbuf_tensor(nm("q32b"), [128, 512], F32) as q32b, \
                 nc.sbuf_tensor(nm("qib"), [128, 512], I32) as qib, \
                 nc.psum_tensor(nm("psd0"), [128, 512], F32) as psd0, \
                 nc.psum_tensor(nm("psd1"), [128, 512], F32) as psd1:
                psd = [psd0, psd1]
                bar = snap()

                @block.sync
                def _(sy):
                    wta(sy, bar)
                    for oc in range(FC):
                        inc(sy.dma_start(
                            out=wdsF[:, oc * ICH * 128:oc * ICH * 128 + 11 * 128],
                            in_=wdq_d[oc, 0:11, :, :].rearrange("i p c -> p i c")), DHs)
                        inc(sy.dma_start(
                            out=wdsF[:, oc * ICH * 128 + 11 * 128:
                                     (oc + 1) * ICH * 128],
                            in_=wdq_d[oc, 11:22, :, :].rearrange("i p c -> p i c")), DHs)
                    sy.wait_ge(DHs, C[DHs])
                mwds = snap()
                omark = {}
                for tc in range(TC):
                    @block.sync
                    def _(sy, tc=tc):
                        if tc >= 1:
                            sy.wait_ge(STs, omark[tc * FC - 1][0])
                        else:
                            wta(sy, bar)
                        for it in range(ICH):
                            inc(sy.dma_start(
                                out=hq[:, it * 512:(it + 1) * 512],
                                in_=hspill[it * 128:(it + 1) * 128,
                                           tc * 512:(tc + 1) * 512]), DHs)
                        sy.wait_ge(DHs, C[DHs])
                    mhq = snap()

                    @block.vector
                    def _(v, tc=tc, mhq=mhq):
                        wta(v, bar)
                        wt(v, DHs, at=mhq)
                        for it in range(ICH):
                            sl = slice(it * 512, (it + 1) * 512)
                            v.tensor_scalar(qib[:, :], hq[:, sl],
                                            invcol[:, 1:2], None, ALU.mult)
                            v.tensor_scalar(hq[:, sl], qib[:, :], -128.0, 127.0,
                                            ALU.max, ALU.min)
                        inc(v.engine_nop(), SVs)
                    qrdy = C[SVs]
                    for oc in range(FC):
                        ostep = tc * FC + oc
                        pp = ostep % 2

                        @block.tensor
                        def _(t, oc=oc, pp=pp, ostep=ostep, qrdy=qrdy, mwds=mwds):
                            wt(t, DHs, at=mwds)
                            t.wait_ge(SVs, qrdy)
                            if ostep >= 2:
                                t.wait_ge(SVs, omark[ostep - 2][1])
                            for it in range(ICH):
                                ins = t.matmul(psd[pp][:, :],
                                         wdsF[:, oc * ICH * 128 + it * 128:
                                              oc * ICH * 128 + it * 128 + 128],
                                         hq[:, it * 512:(it + 1) * 512],
                                         start=(it == 0), stop=(it == ICH - 1))
                            inc(ins, STs)
                            omark[ostep] = [C[STs], 0]

                        @block.vector
                        def _(v, tc=tc, oc=oc, pp=pp, ostep=ostep):
                            v.wait_ge(STs, omark[ostep][0])
                            sl = slice(oc * TOK + tc * 512, oc * TOK + tc * 512 + 512)
                            inc(v.scalar_tensor_tensor(r[:, sl], psd[pp][:, :],
                                                       sdc[:, oc:oc + 1], r[:, sl],
                                                       ALU.mult, ALU.add), SVs)
                            v.wait_ge(SVs, C[SVs])
                            omark[ostep][1] = C[SVs]

        def emit_barrier(cc_idx):
            cin, cout = ccio[cc_idx]
            bar = snap()

            @block.gpsimd
            def _(g):
                wta(g, bar)
                inc(g.collective_compute(
                    "AllReduce", ALU.add, replica_groups=[list(range(NCORES))],
                    ins=[cin], outs=[cout]), CCs)

        def emit_qkv():
            with nc.sbuf_tensor(nm("wsl"), [128, 2 * FC * 128], BF16) as wsl, \
                 nc.sbuf_tensor(nm("pt"), [128, 2 * 512], BF16) as pt, \
                 nc.sbuf_tensor(nm("cct_s"), [128, TOK], BF16) as cct_s, \
                 nc.sbuf_tensor(nm("sst_s"), [128, TOK], BF16) as sst_s, \
                 nc.sbuf_tensor(nm("rtmp"), [128, 512], BF16) as rtmp, \
                 nc.sbuf_tensor(nm("qstg"), [128, 2 * TOK], BF16) as qstg, \
                 nc.sbuf_tensor(nm("wvb"), [128, FC * 1024], BF16) as wvb, \
                 nc.sbuf_tensor(nm("vt"), [128, 2 * 1024], BF16) as vt, \
                 nc.sbuf_tensor(nm("bvrow"), [1, D], F32) as bvrow, \
                 nc.psum_tensor(nm("pq0"), [128, 512], F32) as pq0, \
                 nc.psum_tensor(nm("pq1"), [128, 512], F32) as pq1, \
                 nc.psum_tensor(nm("pva"), [128, 512], F32) as pva, \
                 nc.psum_tensor(nm("pvb"), [128, 512], F32) as pvb, \
                 nc.psum_tensor(nm("pbv"), [128, 512], F32) as pbv, \
                 nc.psum_tensor(nm("qsp0"), [128, 512], F32) as qsp0, \
                 nc.psum_tensor(nm("qsp1"), [128, 512], F32) as qsp1:
                qsp = [qsp0, qsp1]
                pq = [pq0, pq1]
                bar = snap()

                @block.sync
                def _(sy):
                    wta(sy, bar)
                    inc(sy.dma_start(out=cct_s[:], in_=cctab), DHs)
                    inc(sy.dma_start(out=sst_s[:], in_=sstab), DHs)
                    inc(sy.dma_start(out=bvrow[:], in_=bvr), DHs)
                    sy.wait_ge(DHs, C[DHs])
                tabs = snap()

                for h in range(2):
                    m0 = snap()

                    @block.tensor
                    def _(t, h=h, m0=m0):
                        wt(t, DHs, at=tabs)
                        if h == 1:
                            wt(t, SVs, at=m0)
                        inc(t.matmul(pbv[:, :], onf[0:1, 0:128],
                                     bvrow[0:1, h * 512:(h + 1) * 512],
                                     start=True, stop=True), STs)
                    m1 = snap()

                    @block.vector
                    def _(v, h=h, m1=m1):
                        wt(v, STs, at=m1)
                        inc(v.tensor_copy(bv_bc[:, h * 512:(h + 1) * 512], pbv[:, :]), SVs)
                        v.wait_ge(SVs, C[SVs])

                def qk_one(wsrc, bsb, view, ridx_off):
                    tmk, smk, vmk, qmk, wmk = {}, {}, {}, {}, {}
                    seam = snap()
                    for oc in range(FC):
                        ppw = oc % 2

                        @block.gpsimd
                        def _(g, oc=oc, ppw=ppw):
                            wta(g, bar)
                            if oc >= 2:
                                g.wait_ge(STs, tmk.get((oc - 2, TC - 1), 0))
                            inc(g.dma_start(
                                out=wsl[:, ppw * FC * 128:(ppw + 1) * FC * 128],
                                in_=wsrc.rearrange("(f p) c -> p f c", p=128)[
                                    :, :, oc * 128:(oc + 1) * 128]), DGs)
                            g.wait_ge(DGs, C[DGs])
                        mw = snap()
                        for tc in range(TC):
                            n = oc * TC + tc
                            pp = n % 2

                            @block.tensor
                            def _(t, oc=oc, tc=tc, pp=pp, ppw=ppw, mw=mw, n=n):
                                wta(t, bar)
                                wt(t, DGs, at=mw)
                                if n >= 2:
                                    t.wait_ge(SSs, smk.get(n - 2, 0))
                                for fc in range(FC):
                                    ins = t.matmul(pq[pp][:, :],
                                             wsl[:, ppw * FC * 128 + fc * 128:
                                                 ppw * FC * 128 + fc * 128 + 128],
                                             B1[:, fc * TOK + tc * 512:
                                                fc * TOK + tc * 512 + 512],
                                             start=(fc == 0), stop=(fc == FC - 1))
                                inc(ins, STs)
                                tmk[(oc, tc)] = C[STs]

                            @block.scalar
                            def _(s, oc=oc, tc=tc, pp=pp, n=n):
                                s.wait_ge(STs, tmk[(oc, tc)])
                                if n >= 2:
                                    s.wait_ge(SVs, vmk.get(n - 2, 0))
                                    s.wait_ge(STs, qmk.get(n - 2, 0))
                                else:
                                    wta(s, seam)
                                inc(s.activation(pt[:, pp * 512:(pp + 1) * 512],
                                                 pq[pp][:, :], AF.Identity,
                                                 bias=bsb[:, oc:oc + 1], scale=1.0), SSs)
                                smk[n] = C[SSs]

                            @block.tensor
                            def _(t, pp=pp, n=n):
                                t.wait_ge(SSs, smk[n])
                                if n >= 2:
                                    t.wait_ge(SVs, vmk.get(n - 2, 0))
                                inc(t.matmul(qsp[pp][:, :], psw[:, :],
                                             pt[:, pp * 512:(pp + 1) * 512],
                                             start=True, stop=True), STs)
                                qmk[n] = C[STs]

                            @block.vector
                            def _(v, oc=oc, tc=tc, pp=pp, n=n):
                                v.wait_ge(SSs, smk[n])
                                v.wait_ge(STs, qmk[n])
                                if oc >= 2:
                                    v.wait_ge(DHs, wmk.get(oc - 2, 0))
                                else:
                                    wta(v, seam)
                                tsl = slice(tc * 512, (tc + 1) * 512)
                                ppo = oc % 2
                                qsl = slice(ppo * TOK + tc * 512,
                                            ppo * TOK + tc * 512 + 512)
                                v.tensor_mul(rtmp[:, :], pt[:, pp * 512:(pp + 1) * 512],
                                             cct_s[:, tsl])
                                inc(v.tensor_mul(qstg[:, qsl], qsp[pp][:, :],
                                                 sst_s[:, tsl]), SVs)
                                v.wait_ge(SVs, C[SVs])
                                inc(v.tensor_add(qstg[:, qsl], qstg[:, qsl],
                                                 rtmp[:, :]), SVs)
                                vmk[n] = C[SVs]

                        @block.sync
                        def _(sy, oc=oc):
                            sy.wait_ge(SVs, vmk[oc * TC + TC - 1])
                            ppo = oc % 2
                            inc(sy.dma_start(
                                out=view[oc * 128:(oc + 1) * 128, :],
                                in_=qstg[:, ppo * TOK:(ppo + 1) * TOK]), DHs)
                            sy.wait_ge(DHs, C[DHs])
                            wmk[oc] = C[DHs]

                qk_one(wqT, bq_sb, agq_in_q, 0)
                qk_one(wkT, bk_sb, agq_in_k, 0)
                agqk = snap()

                @block.gpsimd
                def _(g):
                    wta(g, agqk)
                    for _i, _o in ((agq_in_q, agq_out_q), (agq_in_k, agq_out_k)):
                        inc(g.collective_compute(
                            "AllGather", ALU.bypass,
                            replica_groups=[[0, 1], [2, 3], [4, 5], [6, 7]],
                            ins=[_i], outs=[_o]), CCs)
                ccqk_mark[0] = C[CCs]

                mv0 = snap()

                @block.gpsimd
                def _(g):
                    wta(g, bar)
                    for fc in range(FC):
                        inc(g.dma_start(out=wvb[:, fc * 1024:(fc + 1) * 1024],
                                        in_=wvT[fc * 128:(fc + 1) * 128, :]), DGs)
                    g.wait_ge(DGs, C[DGs])
                mv1 = snap()
                vtm, vvm, vwm = {}, {}, {}
                for tt in range(NTT):
                    ppv = tt % 2

                    @block.tensor
                    def _(t, tt=tt, mv1=mv1):
                        wt(t, DGs, at=mv1)
                        wt(t, SVs, at=bar)
                        if tt >= 1:
                            t.wait_ge(SVs, vvm[tt - 1])
                        for fc in range(FC):
                            t.matmul(pva[:, :],
                                     B1[:, fc * TOK + tt * 128:fc * TOK + tt * 128 + 128],
                                     wvb[:, fc * 1024:fc * 1024 + 512],
                                     start=(fc == 0), stop=(fc == FC - 1))
                        for fc in range(FC):
                            ins = t.matmul(pvb[:, :],
                                     B1[:, fc * TOK + tt * 128:fc * TOK + tt * 128 + 128],
                                     wvb[:, fc * 1024 + 512:(fc + 1) * 1024],
                                     start=(fc == 0), stop=(fc == FC - 1))
                        inc(ins, STs)
                        vtm[tt] = C[STs]

                    @block.vector
                    def _(v, tt=tt, ppv=ppv):
                        v.wait_ge(STs, vtm[tt])
                        if tt >= 2:
                            v.wait_ge(DHs, vwm[tt - 2])
                        inc(v.tensor_add(vt[:, ppv * 1024:ppv * 1024 + 512], pva[:, :],
                                         bv_bc[:, 0:512]), SVs)
                        inc(v.tensor_add(vt[:, ppv * 1024 + 512:(ppv + 1) * 1024],
                                         pvb[:, :], bv_bc[:, 512:1024]), SVs)
                        v.wait_ge(SVs, C[SVs])
                        vvm[tt] = C[SVs]

                    @block.sync
                    def _(sy, tt=tt, ppv=ppv):
                        sy.wait_ge(SVs, vvm[tt])
                        for gg in range(FC):
                            inc(sy.dma_start(
                                out=agq_in_v[gg * TOK + tt * 128:
                                             gg * TOK + (tt + 1) * 128, :],
                                in_=vt[:, ppv * 1024 + gg * 128:
                                       ppv * 1024 + (gg + 1) * 128]), DHs)
                        sy.wait_ge(DHs, C[DHs])
                        vwm[tt] = C[DHs]

        def emit_attention():
            with nc.sbuf_tensor(nm("kb"), [128, 2 * T], BF16) as kb, \
                 nc.sbuf_tensor(nm("qb"), [128, 2 * T], BF16) as qb, \
                 nc.sbuf_tensor(nm("vb"), [128, 2 * T], BF16) as vb, \
                 nc.sbuf_tensor(nm("sqt"), [128, 512], BF16) as sqt, \
                 nc.sbuf_tensor(nm("ptA"), [128, 2 * 512], BF16) as ptA, \
                 nc.sbuf_tensor(nm("ptB"), [128, 2 * 512], BF16) as ptB, \
                 nc.sbuf_tensor(nm("den"), [33, 512], F32) as den, \
                 nc.sbuf_tensor(nm("rec"), [128, 512], F32) as rec, \
                 nc.sbuf_tensor(nm("ajb"), [128, 2 * T], BF16) as ajb, \
                 nc.sbuf_tensor(nm("mrow"), [1, 8], F32) as mrow, \
                 nc.sbuf_tensor(nm("negm"), [128, 1], F32) as negm, \
                 nc.psum_tensor(nm("psA0"), [128, 512], F32) as psA0, \
                 nc.psum_tensor(nm("psB0"), [128, 512], F32) as psB0, \
                 nc.psum_tensor(nm("psA1"), [128, 512], F32) as psA1, \
                 nc.psum_tensor(nm("psB1"), [128, 512], F32) as psB1, \
                 nc.psum_tensor(nm("ppA"), [128, 512], F32) as ppA, \
                 nc.psum_tensor(nm("ppB"), [128, 512], F32) as ppB:
                _aes = ExitStack()
                pdA = _aes.enter_context(nc.psum_tensor(nm("pdA"), [128, 512], F32))
                pdB = _aes.enter_context(nc.psum_tensor(nm("pdB"), [128, 512], F32))
                psAp = [psA0, psA1]; psBp = [psB0, psB1]
                bar = snap()

                @block.vector
                def _(v):
                    wta(v, bar)
                    inc(v.memset(den[:], 1.0), SVs)
                ldm, jobdone, vbm = {}, {}, {}
                scmk, scmkA, exmk, mkmk, pvmk = {}, {}, {}, {}, {}
                gkc = [0]
                mjs = {}

                def emit_ago(j):
                    pj = j % 2
                    mj = mjs[j]

                    @block.gpsimd
                    def _(g, j=j, pj=pj, mj=mj):
                        wt(g, SVs, at=mj)
                        inc(g.dma_start(out=ago_in[j * 128:(j + 1) * 128, :],
                                        in_=ajb[:, pj * T:(pj + 1) * T]), DGs)
                        if j == 0 and stage in (2, 16):
                            inc(g.dma_start(out=dbg9, in_=ajb[:, 0:1024]), DGs)
                            inc(g.dma_start(out=dbg6, in_=kb[:, 0:512]), DGs)
                            inc(g.dma_start(out=dbg7, in_=qb[:, 0:512]), DGs)
                            inc(g.dma_start(out=dbg10, in_=vb[:, 0:512]), DGs)
                        g.wait_ge(DGs, C[DGs])
                        inc(g.collective_compute(
                            "AllGather", ALU.bypass,
                            replica_groups=[[0, 1], [2, 3], [4, 5], [6, 7]],
                            ins=[ago_in[j * 128:(j + 1) * 128, :]],
                            outs=[ago_outs[j]]), CCs)

                for j in range(4):
                    pj = j % 2

                    @block.gpsimd
                    def _(g, j=j, pj=pj):
                        for _s in (DHs, DGs, SVs, SSs, STs):
                            if bar[_s] > 0:
                                g.wait_ge(_s, bar[_s])
                        g.wait_ge(CCs, ccqk_mark[0])
                        if j >= 2:
                            g.wait_ge(STs, jobdone.get(j - 2, 0))
                        for rr in range(2):
                            ii = IX_JRK + 2 * j + rr
                            inc(g.indirect_dma_start(
                                kb[:, pj * T + rr * TOK:pj * T + (rr + 1) * TOK], None,
                                agq_out_k,
                                IndirectOffsetOnAxis(idxm[:, ii:ii + 1], 0)), DGs)
                            inc(g.indirect_dma_start(
                                qb[:, pj * T + rr * TOK:pj * T + (rr + 1) * TOK], None,
                                agq_out_q,
                                IndirectOffsetOnAxis(idxm[:, ii:ii + 1], 0)), DGs)
                        g.wait_ge(DGs, C[DGs])
                    gqkm = C[DGs]
                    ldm[j] = C[DGs]

                    @block.gpsimd
                    def _(g, j=j, pj=pj):
                        g.wait_ge(CCs, bar[CCs])
                        if j >= 2:
                            g.wait_ge(STs, jobdone.get(j - 2, 0))
                        for cch in range(SEQC):
                            inc(g.indirect_dma_start(
                                vb[:, pj * T + cch * 128:pj * T + (cch + 1) * 128], None,
                                agq_out_v,
                                IndirectOffsetOnAxis(
                                    idxm[:, IX_VR + 4 * cch + j:IX_VR + 4 * cch + j + 1],
                                    0)), DGs)
                        g.wait_ge(DGs, C[DGs])
                    vbm[j] = C[DGs]

                    if j >= 1:
                        emit_ago(j - 1)

                    nmk = {}
                    mq0 = snap()

                    @block.vector
                    def _(v, j=j, mq0=mq0, gqkm=gqkm):
                        v.wait_ge(DGs, gqkm)
                        inc(v.memset(mrow[0:1, 0:2], 0.0), SVs)
                    for (src, col) in ((qb, 0), (kb, 1)):
                        for tcc in range(8):
                            mq1 = snap()

                            @block.scalar
                            def _(s, src=src, tcc=tcc, pj=pj, mq1=mq1, gqkm=gqkm):
                                wt(s, SVs, at=mq1)
                                s.wait_ge(DGs, gqkm)
                                if tcc >= 1 or src is kb:
                                    wt(s, STs)
                                inc(s.activation(sqt[:, :],
                                                 src[:, pj * T + tcc * 512:
                                                     pj * T + (tcc + 1) * 512],
                                                 AF.Square), SSs)
                            mq2 = snap()

                            @block.tensor
                            def _(t, mq2=mq2):
                                wt(t, SSs, at=mq2)
                                wt(t, SVs)
                                inc(t.matmul(pdA[0:1, :], onb[:, 0:1], sqt[:, :],
                                             start=True, stop=True), STs)
                            mq3 = snap()

                            @block.vector
                            def _(v, col=col, mq3=mq3):
                                wt(v, STs, at=mq3)
                                inc(v.reduce_max(mrow[0:1, 2:3], pdA[0:1, :],
                                                 axis=AXX), SVs)
                                v.wait_ge(SVs, C[SVs])
                                inc(v.tensor_max(mrow[0:1, col:col + 1],
                                                 mrow[0:1, col:col + 1],
                                                 mrow[0:1, 2:3]), SVs)
                    mq4 = snap()

                    @block.vector
                    def _(v, mq4=mq4):
                        wt(v, SVs, at=mq4)
                        inc(v.tensor_mul(mrow[0:1, 3:4], mrow[0:1, 0:1],
                                         mrow[0:1, 1:2]), SVs)
                    mq5 = snap()

                    @block.tensor
                    def _(t, mq5=mq5):
                        wt(t, SVs, at=mq5)
                        inc(t.matmul(pdA[:, 0:1], onf[0:1, 0:128], mrow[0:1, 3:4],
                                     start=True, stop=True), STs)
                    mq6 = snap()

                    @block.scalar
                    def _(s, mq6=mq6):
                        wt(s, STs, at=mq6)
                        inc(s.activation(negm[:, 0:1], pdA[:, 0:1], AF.Sqrt,
                                         bias=0.0, scale=1.0 / 64.0), SSs)
                    mq7 = snap()

                    @block.vector
                    def _(v, mq7=mq7):
                        wt(v, SSs, at=mq7)
                        inc(v.tensor_scalar(negm[:, 0:1], negm[:, 0:1], -1.0, None,
                                            ALU.mult), SVs)
                    mbound = snap()

                    for tcc in range(8):
                        njj = 4 * tcc + 4
                        for k in range(njj + 1):
                            @block.tensor
                            def _(t, j=j, pj=pj, tcc=tcc, k=k, njj=njj,
                                  mbound=mbound, gk=gkc[0]):
                                if k < njj:
                                    par = (gk + k) % 2
                                    wt(t, SVs, at=mbound)
                                    t.wait_ge(SSs, exmk.get(gk + k - 2, 0))
                                    tq = slice(pj * T + tcc * 512,
                                               pj * T + (tcc + 1) * 512)
                                    inc(t.matmul(psAp[par][:, :],
                                             kb[0:64, pj * T + k * 128:
                                                pj * T + k * 128 + 128],
                                             qb[0:64, tq], start=True, stop=True,
                                             tile_position=(0, 0)), STs)
                                    scmkA[gk + k] = C[STs]
                                    inc(t.matmul(psBp[par][:, :],
                                                 kb[64:128, pj * T + k * 128:
                                                    pj * T + k * 128 + 128],
                                                 qb[64:128, tq], start=True, stop=True,
                                                 tile_position=(64, 0)), STs)
                                    scmk[gk + k] = C[STs]
                                if k >= 1:
                                    jj = k - 1
                                    par = (gk + jj) % 2
                                    if tcc == 0 and jj == 0:
                                        t.wait_ge(DGs, vbm[j])
                                    if jj - 4 * tcc >= 0:
                                        t.wait_ge(SVs, mkmk[gk + jj])
                                    else:
                                        t.wait_ge(SSs, exmk[gk + jj])
                                    st_ = (jj == 0)
                                    sp_ = (jj == njj - 1)
                                    vsl = slice(pj * T + jj * 128,
                                                pj * T + jj * 128 + 128)
                                    pa = ptA[:, par * 512:(par + 1) * 512]
                                    pb = ptB[:, par * 512:(par + 1) * 512]
                                    t.matmul(ppA[0:64, :], vb[:, vsl][:, 0:64], pa,
                                             start=st_, stop=sp_, tile_position=(0, 0))
                                    t.matmul(ppB[64:128, :], vb[:, vsl][:, 64:128], pb,
                                             start=st_, stop=sp_, tile_position=(0, 64))
                                    t.matmul(pdA[0:1, :], onb[:, 0:1], pa,
                                             start=st_, stop=sp_, tile_position=(0, 0))
                                    inc(t.matmul(pdB[32:33, :], onb[:, 1:2], pb,
                                                 start=st_, stop=sp_,
                                                 tile_position=(0, 32)), STs)
                                    pvmk[gk + jj] = C[STs]

                            if k < njj:
                                @block.scalar
                                def _(s, k=k, gk=gkc[0]):
                                    par = (gk + k) % 2
                                    s.wait_ge(STs, scmkA[gk + k])
                                    s.wait_ge(STs, pvmk.get(gk + k - 2, 0))
                                    s.activation(ptA[:, par * 512:(par + 1) * 512],
                                                 psAp[par][:, :], AF.Exp,
                                                 bias=negm[:, 0:1], scale=0.125)
                                    s.wait_ge(STs, scmk[gk + k])
                                    inc(s.activation(ptB[:, par * 512:(par + 1) * 512],
                                                     psBp[par][:, :], AF.Exp,
                                                     bias=negm[:, 0:1], scale=0.125),
                                        SSs)
                                    exmk[gk + k] = C[SSs]

                                diag = k - 4 * tcc
                                if diag >= 0:
                                    @block.vector
                                    def _(v, k=k, diag=diag, gk=gkc[0]):
                                        par = (gk + k) % 2
                                        v.wait_ge(SSs, exmk[gk + k])
                                        v.tensor_mul(
                                            ptA[:, par * 512:(par + 1) * 512],
                                            ptA[:, par * 512:(par + 1) * 512],
                                            msk[:, diag * 512:(diag + 1) * 512])
                                        inc(v.tensor_mul(
                                            ptB[:, par * 512:(par + 1) * 512],
                                            ptB[:, par * 512:(par + 1) * 512],
                                            msk[:, diag * 512:(diag + 1) * 512]), SVs)
                                        v.wait_ge(SVs, C[SVs])
                                        mkmk[gk + k] = C[SVs]
                        gkc[0] += njj
                        mpv = snap()

                        @block.vector
                        def _(v, mpv=mpv):
                            wt(v, STs, at=mpv)
                            inc(v.tensor_copy(den[0:1, :], pdA[0:1, :]), SVs)
                            inc(v.tensor_copy(den[32:33, :], pdB[32:33, :]), SVs)
                            v.wait_ge(SVs, C[SVs])
                            inc(v.reciprocal(den[0:1, :], den[0:1, :]), SVs)
                            inc(v.reciprocal(den[32:33, :], den[32:33, :]), SVs)
                            v.wait_ge(SVs, C[SVs])
                            inc(v.engine_nop(), SVs)
                        mdn = snap()

                        @block.tensor
                        def _(t, mdn=mdn):
                            wt(t, SVs, at=mdn)
                            wt(t, SVs)
                            t.matmul(pdA[0:64, :], onf[0:1, 0:64], den[0:1, :],
                                     start=True, stop=True, tile_position=(0, 0))
                            inc(t.matmul(pdB[64:128, :], onf[32:33, 0:64], den[32:33, :],
                                         start=True, stop=True,
                                         tile_position=(32, 64)), STs)
                        mrc = snap()

                        @block.vector
                        def _(v, j=j, pj=pj, tcc=tcc, mrc=mrc):
                            wt(v, STs, at=mrc)
                            wt(v, DGs)
                            inc(v.tensor_copy(rec[0:64, :], pdA[0:64, :]), SVs)
                            inc(v.tensor_copy(rec[64:128, :], pdB[64:128, :]), SVs)
                            v.wait_ge(SVs, C[SVs])
                            asl = slice(pj * T + tcc * 512, pj * T + (tcc + 1) * 512)
                            inc(v.tensor_mul(ajb[0:64, asl], ppA[0:64, :], rec[0:64, :]), SVs)
                            inc(v.tensor_mul(ajb[64:128, asl], ppB[64:128, :],
                                             rec[64:128, :]), SVs)
                            v.wait_ge(SVs, C[SVs])
                    mjs[j] = snap()
                    jobdone[j] = C[STs]
                emit_ago(3)
                _aes.close()

        def emit_oproj():
            with nc.sbuf_tensor(nm("wsl2"), [128, 2 * FC * 128], BF16) as wsl2, \
                 nc.psum_tensor(nm("po0"), [128, 512], F32) as po0, \
                 nc.psum_tensor(nm("po1"), [128, 512], F32) as po1:
                po = [po0, po1]
                bar = snap()

                @block.gpsimd
                def _(g):
                    wta(g, bar)
                    for fc in range(FC):
                        inc(g.indirect_dma_start(
                            B1[:, fc * TOK:(fc + 1) * TOK], None, ar_views[fc % 4],
                            IndirectOffsetOnAxis(idxm[:, IX_QKW + fc:IX_QKW + fc + 1],
                                                 0)), DGs)
                    g.wait_ge(DGs, C[DGs])
                mrd = snap()
                tmk, vmk = {}, {}
                for oc in range(FC):
                    ppw = oc % 2

                    @block.gpsimd
                    def _(g, oc=oc, ppw=ppw):
                        if oc >= 2:
                            g.wait_ge(STs, tmk.get((oc - 2, TC - 1), 0))
                        else:
                            wta(g, bar)
                        inc(g.dma_start(
                            out=wsl2[:, ppw * FC * 128:(ppw + 1) * FC * 128],
                            in_=woT.rearrange("(f p) c -> p f c", p=128)[
                                :, :, oc * 128:(oc + 1) * 128]), DGs)
                        g.wait_ge(DGs, C[DGs])
                    mw = snap()
                    for tc in range(TC):
                        n = oc * TC + tc
                        pp = n % 2

                        @block.tensor
                        def _(t, oc=oc, tc=tc, pp=pp, ppw=ppw, mw=mw, n=n, mrd=mrd):
                            wt(t, DGs, at=mw)
                            wt(t, DGs, at=mrd)
                            if n >= 2:
                                t.wait_ge(SVs, vmk.get(n - 2, 0))
                            for fc in range(FC):
                                ins = t.matmul(po[pp][:, :],
                                         wsl2[:, ppw * FC * 128 + fc * 128:
                                             ppw * FC * 128 + fc * 128 + 128],
                                         B1[:, fc * TOK + tc * 512:
                                            fc * TOK + tc * 512 + 512],
                                         start=(fc == 0), stop=(fc == FC - 1))
                            inc(ins, STs)
                            tmk[(oc, tc)] = C[STs]

                        @block.vector
                        def _(v, oc=oc, tc=tc, pp=pp, n=n):
                            v.wait_ge(STs, tmk[(oc, tc)])
                            sl = slice(oc * TOK + tc * 512, oc * TOK + tc * 512 + 512)
                            inc(v.scalar_tensor_tensor(r[:, sl], po[pp][:, :],
                                                       bo_sb[:, oc:oc + 1], r[:, sl],
                                                       ALU.add, ALU.add), SVs)
                            v.wait_ge(SVs, C[SVs])
                            vmk[n] = C[SVs]

        return_early = [False]

        def _stage0_store():
            bar = snap()

            @block.sync
            def _(sy):
                wta(sy, bar)
                for fc in range(FC):
                    inc(sy.dma_start(out=out[fc * 128:(fc + 1) * 128, :],
                                     in_=r[:, fc * TOK:(fc + 1) * TOK]), DHs)
                inc(sy.dma_start(out=dbg[:, 0:8], in_=srow[:, 0:8],
                                 allow_slow_non_contiguous=True), DHs)
                inc(sy.dma_start(out=dbg[:, 8:13], in_=abc[:, 0:5],
                                 allow_slow_non_contiguous=True), DHs)
                inc(sy.dma_start(out=dbg4[:, :], in_=B1[:, 0:512]), DHs)
                wt(sy, DHs)

        ccqk_mark = [0]
        emit_ln()
        emit_gmax(0, 0)
        if stage == 0:
            bar0 = snap()

            @block.vector
            def _(v):
                wta(v, bar0)
                v.tensor_copy(srow[:, 7:8], umax[:, 0:1])
                v.tensor_copy(abc[0:1, 0:2], lmax[0:1, 0:2])
                inc(v.tensor_copy(abc[0:1, 4:5], gmv[0:1, 0:1]), SVs)
            _stage0_store()
            return_early[0] = True
        if not return_early[0]:
            emit_weight_prep()
            emit_scale_cols(swg, ICH, sgc, 1.0)
            emit_scale_cols(swu, ICH, suc, 1.0)
            emit_quant_B1(0)
            emit_ffn_a()
            emit_gmax(1, 1)
            emit_scale_cols(swd, FC, sdc, 0.5)
            emit_ffn_b()

        if not return_early[0] and stage >= 2:
            emit_ln()
            emit_qkv()
            bar_ag = snap()

            @block.gpsimd
            def _(g):
                wta(g, bar_ag)
                inc(g.collective_compute(
                    "AllGather", ALU.bypass,
                    replica_groups=[[0, 1], [2, 3], [4, 5], [6, 7]],
                    ins=[agq_in_v], outs=[agq_out_v]), CCs)
            if stage == 15:
                bar15 = snap()

                @block.gpsimd
                def _(g):
                    wta(g, bar15)
                    inc(g.dma_start(out=dbg6, in_=agq_out_q[0:128, 0:512]), DGs)
                    inc(g.dma_start(out=dbg7, in_=agq_out_k[8 * 128:9 * 128, 0:512]), DGs)
                    for _k in range(4):
                        inc(g.dma_start(out=dbg2[:, _k * 128:(_k + 1) * 128],
                                        in_=agq_out_v[_k * 128:(_k + 1) * 128, :]), DGs)
            if stage >= 2 and stage != 15:
                emit_attention()
                if stage != 16:
                    emit_oproj()

        if not return_early[0] and stage >= 3 and stage != 15 and stage != 16:
            emit_ln()
            emit_gmax(4, 0)
            emit_scale_cols(swg, ICH, sgc, 1.0)
            emit_scale_cols(swu, ICH, suc, 1.0)
            emit_quant_B1(0)
            emit_ffn_a()
            emit_gmax(5, 1)
            emit_scale_cols(swd, FC, sdc, 0.5)
            emit_ffn_b()

        if return_early[0]:
            return nc
        bar = snap()

        @block.sync
        def _(sy):
            wta(sy, bar)
            for fc in range(FC):
                inc(sy.dma_start(out=out[fc * 128:(fc + 1) * 128, :],
                                 in_=r[:, fc * TOK:(fc + 1) * TOK]), DHs)
            wt(sy, DHs)
    return nc


def _host_inputs(inputs, core):
    b, sig = core // 2, core % 2
    x = np.asarray(inputs["L0_attn_x"])
    xs = np.ascontiguousarray(x[b, sig * TOK:(sig + 1) * TOK, :].T)

    perm = np.zeros(D, np.int64)
    for h in range(NH):
        perm[h * 64:h * 64 + 32] = h * 64 + 2 * np.arange(32)
        perm[h * 64 + 32:h * 64 + 64] = h * 64 + 2 * np.arange(32) + 1
    Wq = np.asarray(inputs["L0_attn_W_q"])[perm, :]
    Wk = np.asarray(inputs["L0_attn_W_k"])[perm, :]
    bq = np.asarray(inputs["L0_attn_b_q"])[perm]
    bk = np.asarray(inputs["L0_attn_b_k"])[perm]
    Wv = np.asarray(inputs["L0_attn_W_v"])
    Wo = np.asarray(inputs["L0_attn_W_o"])
    bv = np.asarray(inputs["L0_attn_b_v"])
    bo = np.asarray(inputs["L0_attn_b_o"])
    wg = np.asarray(inputs["L0_ffn_w_gate"]); wu = np.asarray(inputs["L0_ffn_w_up"])
    wd = np.asarray(inputs["L0_ffn_w_down"])

    t = np.arange(sig * TOK, (sig + 1) * TOK, dtype=np.float64)
    theta = (1.0 / 10000.0) ** (np.arange(32, dtype=np.float64) / 32.0)
    ang = np.float32(np.arange(sig * TOK, (sig + 1) * TOK, dtype=np.float32)[None, :]
                     * np.tile(theta, 4).astype(np.float32)[:, None]).astype(np.float64)
    ang32 = (np.arange(sig * TOK, (sig + 1) * TOK, dtype=np.float32)[None, :]
             * np.tile(theta.astype(np.float32), 4)[:, None]).astype(np.float32)
    cc = np.cos(ang32.astype(np.float64)).astype(ml_dtypes.bfloat16)
    sgn = np.repeat(np.array([-1.0, 1.0, -1.0, 1.0]), 32)[:, None]
    ss = (sgn * np.sin(ang32.astype(np.float64))).astype(ml_dtypes.bfloat16)

    mask = np.zeros((128, 4 * 512), dtype=ml_dtypes.bfloat16)
    p = np.arange(128)[:, None]
    for m in range(4):
        mask[:, m * 512:(m + 1) * 512] = (128 * m + p <= np.arange(512)[None, :])

    pcol = np.arange(128, dtype=np.int32)[:, None]
    itbl = np.concatenate([2 * pcol + sig,
                           sig * 2048 + pcol,
                           512 * sig + pcol,
                           8192 * sig + pcol], axis=1).astype(np.int32)

    return dict(
        xT=xs,
        wgT=np.ascontiguousarray(wg.T), wuT=np.ascontiguousarray(wu.T),
        wdT=np.ascontiguousarray(wd.T),
        wg_nat=np.ascontiguousarray(wg), wu_nat=np.ascontiguousarray(wu),
        wd_nat=np.ascontiguousarray(wd),
        wqT=np.ascontiguousarray(Wq.T), wkT=np.ascontiguousarray(Wk.T),
        wvT=np.ascontiguousarray(Wv.T), woT=np.ascontiguousarray(Wo.T),
        bqc=np.ascontiguousarray(bq.reshape(FC, 128).T.astype(np.float32)),
        bkc=np.ascontiguousarray(bk.reshape(FC, 128).T.astype(np.float32)),
        boc=np.ascontiguousarray(bo.reshape(FC, 128).T.astype(np.float32)),
        bvr=np.ascontiguousarray(bv[None, :].astype(np.float32)),
        clip2=np.array([[float(np.asarray(inputs["L0_ffn_clip_x"]).ravel()[0]),
                         float(np.asarray(inputs["L0_ffn_clip_down"]).ravel()[0])]],
                       np.float32),
        cctab=np.ascontiguousarray(cc), sstab=np.ascontiguousarray(ss),
        masks=mask,
        identf=np.eye(128, dtype=np.float32),
        permb=np.ascontiguousarray(np.kron(np.eye(2), np.kron(
            np.array([[0, 1], [1, 0]]), np.eye(32))).astype(ml_dtypes.bfloat16)),
        ones_b=np.ones((128, 128), dtype=ml_dtypes.bfloat16),
        ones_f=np.ones((128, 128), dtype=np.float32),
        itbl=itbl,
    )


def kernel(**inputs):
    nc = build(STAGE)
    in_maps = [_host_inputs(inputs, c) for c in range(NCORES)]
    res = run_bass_kernel_spmd(nc, in_maps, core_ids=list(range(NCORES)))
    full = np.zeros((B, T, D), np.float32)
    for c in range(NCORES):
        b, sig = c // 2, c % 2
        full[b, sig * TOK:(sig + 1) * TOK, :] = res.results[c]["out"].T
    return full
